# version 01
# baseline (speedup 1.0000x reference)
import numpy as np
import ml_dtypes
from contextlib import ExitStack

import concourse.bass as bass
import concourse.mybir as mybir
from concourse.bass_utils import run_bass_kernel_spmd
from concourse.bass import IndirectOffsetOnAxis

F32 = mybir.dt.float32
BF16 = mybir.dt.bfloat16
U32 = mybir.dt.uint32
AF = mybir.ActivationFunctionType
ALU = mybir.AluOpType
AX = mybir.AxisListType

NCORE = 8
NCH = 16
CT = 128
HALO = 32
W = 192
DM = 1024
FC = 8
NH = 16
HD = 64
E = 8
P = 512
NSLOT = E * P
CAPF = 2560.0
TOK = NCH * CT
SEQ = 4096

DEBUG = False
PHASES = 5
DBG_CHUNK = -1
CBLOCKS = 8
CPART = 2

ENGS = ("sync", "gpsimd", "tensor", "vector", "scalar")

CONSTS = [("zero", 0.0), ("one", 1.0), ("negone", -1.0),
          ("eps", 1e-5), ("epsr", float(np.finfo(np.float32).eps)),
          ("i1024", 1.0 / 1024.0), ("cap", CAPF), ("capP", float(P)),
          ("huge", 1e9), ("inv8", 0.125)]


class Prog:
    def __init__(self, nc, stack):
        self.nc = nc
        names = ["LD", "WD", "ST", "TE", "VE", "SE", "CCS"]
        self.h = {n: stack.enter_context(nc.semaphore(n)) for n in names}
        self.c = {n: 0 for n in names}
        self.ops = {e: [] for e in ENGS}

    def on(self, eng, fn):
        self.ops[eng].append(fn)

    def step(self, eng, sem, fn, amt=1):
        h = self.h[sem]
        if eng in ("vector", "scalar"):
            self.ops[eng].append(lambda e, fn=fn: fn(e))
            self.ops[eng].append(lambda e, h=h, amt=amt: e.drain().then_inc(h, amt))
        else:
            self.ops[eng].append(lambda e, fn=fn, h=h, amt=amt: fn(e).then_inc(h, amt))
        self.c[sem] += amt
        return self.c[sem]

    def step_acc(self, eng, sem, fn, amt=1):
        return self.step(eng, sem, fn, amt=amt)

    def bar(self, eng):
        self.ops[eng].append(lambda e: e.drain())

    def stepb(self, eng, sem, fn, amt=1):
        self.bar(eng)
        return self.step(eng, sem, fn, amt=amt)

    def dma(self, eng, sem, fn):
        return self.step(eng, sem, fn, amt=16)

    def wait(self, eng, sem, val):
        if val <= 0:
            return
        h = self.h[sem]
        self.ops[eng].append(lambda e, h=h, val=val: e.wait_ge(h, val))

    def flush(self, block):
        for eng in ENGS:
            fns = self.ops[eng]
            if not fns:
                continue
            self.ops[eng] = []
            getattr(block, eng)(lambda e, fns=fns: [f(e) for f in fns])


def build_nc(debug=False):
    nc = bass.Bass("TRN2", target_bir_lowering=False)

    D = {}
    def din(name, shape, dt):
        D[name] = nc.dram_tensor(name, shape, dt, kind="ExternalInput").ap()
    din("xw", [NCH, W, DM], F32)
    din("kvm", [NCH, W], F32)
    din("kvmt", [W, NCH], F32)
    for wn in ("qw", "kw", "vw", "ow"):
        din(wn, [128, FC, DM], F32)
    for bn_ in ("qb", "kb", "vb", "ob"):
        din(bn_, [1, DM], F32)
    din("rw", [128, FC, E], F32)
    din("sg", [8, 128, FC, 1024], F32)
    din("su", [8, 128, FC, 1024], F32)
    din("sd", [8, 128, FC, 1024], F32)
    din("rg", [E, 4, 128, FC, 1024], F32)
    din("ru", [E, 4, 128, FC, 1024], F32)
    din("rd", [E, 4, 128, FC, 1024], F32)
    din("identf", [128, 128], F32)
    din("identb", [128, 128], BF16)
    din("tri", [128, 128], BF16)
    din("onesf", [1, 128], F32)
    din("onesc", [128, 1], BF16)
    din("iota8", [128, E], F32)
    din("bm", [128, W], BF16)
    din("pickb", [NCH, 128, 128], F32)

    out_d = nc.dram_tensor("out", [NCH, CT, DM], F32, kind="ExternalOutput").ap()

    ikind = {"kind": "ExternalOutput"} if debug else {}
    x2_d = nc.dram_tensor("x2d", [NCH, CT, DM], F32, **ikind).ap()
    xg_d = nc.dram_tensor("xgd", [NSLOT, DM], BF16, **ikind).ap()
    eo_d = nc.dram_tensor("eod", [NSLOT, DM], BF16, **ikind).ap()
    cc_i = nc.dram_tensor("cci", [NCH, E], F32).ap()
    cc_o = nc.dram_tensor("cco", [128, E], F32).ap()
    if debug:
        dbg_off = nc.dram_tensor("dbgoff", [128, NCH * 4], F32, kind="ExternalOutput").ap()
        dbg_w = nc.dram_tensor("dbgw", [128, NCH * 2], F32, kind="ExternalOutput").ap()
        dbg_xfn = nc.dram_tensor("dbgxfn", [NCH, CT, DM], BF16, kind="ExternalOutput").ap()
        dbg_cag = nc.dram_tensor("dbgcag", [128, E], F32, kind="ExternalOutput").ap()
        dbg_dall = nc.dram_tensor("dbgdall", [128, NCH], F32, kind="ExternalOutput").ap()
        dbg_acc = nc.dram_tensor("dbgacc", [NCH, CT, DM], F32, kind="ExternalOutput").ap()
        dbg_xlnf = nc.dram_tensor("dbgxlnf", [128, FC * W], BF16, kind="ExternalOutput").ap()
        dbg_qfm = nc.dram_tensor("dbgqfm", [128, FC * CT], BF16, kind="ExternalOutput").ap()
        dbg_kfm = nc.dram_tensor("dbgkfm", [128, FC * W], BF16, kind="ExternalOutput").ap()
        dbg_vt = nc.dram_tensor("dbgvt", [128, 2 * DM], BF16, kind="ExternalOutput").ap()
        dbg_at = nc.dram_tensor("dbgat", [128, FC * CT], BF16, kind="ExternalOutput").ap()
        dbg_pn = nc.dram_tensor("dbgpn", [128, NH * CT], BF16, kind="ExternalOutput").ap()
        dbg_xln = nc.dram_tensor("dbgxln", [128, 3 * DM], F32, kind="ExternalOutput").ap()
        dbg_stat = nc.dram_tensor("dbgstat", [128, 16], F32, kind="ExternalOutput").ap()

    st0 = ExitStack()
    with st0:
        ev = st0.enter_context
        S = Prog(nc, st0)
        blk = ev(nc.Block())

        identf = ev(nc.sbuf_tensor("identf_s", [128, 128], F32))
        identb = ev(nc.sbuf_tensor("identb_s", [128, 128], BF16))
        tri = ev(nc.sbuf_tensor("tri_s", [128, 128], BF16))
        onesf = ev(nc.sbuf_tensor("onesf_s", [1, 128], F32))
        onesc = ev(nc.sbuf_tensor("onesc_s", [128, 1], BF16))
        iota8 = ev(nc.sbuf_tensor("iota8_s", [128, E], F32))
        bm = ev(nc.sbuf_tensor("bm_s", [128, W], BF16))
        pickb = ev(nc.sbuf_tensor("pickb_s", [128, NCH * 128], F32))
        kvmT_a = ev(nc.sbuf_tensor("kvmTa", [128, NCH], F32))
        kvmT_b = ev(nc.sbuf_tensor("kvmTb", [128, NCH], F32))
        kvmrow = ev(nc.sbuf_tensor("kvmrow", [1, NCH * W], BF16))
        qb = ev(nc.sbuf_tensor("qb_s", [1, DM], BF16))
        kb = ev(nc.sbuf_tensor("kb_s", [1, DM], BF16))
        vb = ev(nc.sbuf_tensor("vb_s", [1, DM], BF16))
        ob = ev(nc.sbuf_tensor("ob_s", [1, DM], BF16))
        rwT = ev(nc.sbuf_tensor("rwT_s", [128, FC * E], F32))
        consts = {}
        for cn, _cv in CONSTS:
            consts[cn] = ev(nc.sbuf_tensor("c_" + cn, [128, 1], F32))
        cntrow = ev(nc.sbuf_tensor("cntrow", [1, 128], F32))
        counts_ag = ev(nc.sbuf_tensor("counts_ag", [128, E], F32))
        runoff = ev(nc.sbuf_tensor("runoff", [1, E], F32))
        sel_all = ev(nc.sbuf_tensor("sel_all", [128, NCH * E], BF16))
        m1_all = ev(nc.sbuf_tensor("m1_all", [128, NCH * E], BF16))
        m2_all = ev(nc.sbuf_tensor("m2_all", [128, NCH * E], BF16))
        rank_all = ev(nc.sbuf_tensor("rank_all", [128, NCH * E], F32))
        dall = ev(nc.sbuf_tensor("dall", [128, NCH], F32))
        offs_f = ev(nc.sbuf_tensor("offs_f", [128, NCH * 4], F32))
        offs_u = ev(nc.sbuf_tensor("offs_u", [128, NCH * 4], U32))
        w01 = ev(nc.sbuf_tensor("w01", [128, NCH * 2], F32))

        for dst, src in [(identf, D["identf"]), (identb, D["identb"]),
                         (tri, D["tri"]), (onesf, D["onesf"]), (onesc, D["onesc"]), (iota8, D["iota8"]),
                         (bm, D["bm"])]:
            S.dma("sync", "LD", lambda e, d=dst, s=src: e.dma_start(out=d[:], in_=s[:]))
        S.dma("sync", "LD", lambda e: e.dma_start(
            out=pickb[:].rearrange("p (c m) -> p c m", m=128),
            in_=D["pickb"].rearrange("c p m -> p c m")))
        S.dma("sync", "LD", lambda e: e.dma_start(out=kvmT_a[:], in_=D["kvmt"][0:128, :]))
        S.dma("sync", "LD", lambda e: e.dma_start(out=kvmT_b[0:64, :], in_=D["kvmt"][128:192, :]))
        S.dma("sync", "LD", lambda e: e.dma_start(
            out=rwT[:], in_=D["rw"].rearrange("p c e -> p (c e)")))
        kvmflat = D["kvm"].rearrange("c w -> (c w)")[None, :]
        S.dma("gpsimd", "WD", lambda e, s=kvmflat: e.dma_start(out=kvmrow[:], in_=s))
        for dst, src in [(qb, D["qb"]), (kb, D["kb"]), (vb, D["vb"]), (ob, D["ob"])]:
            S.dma("gpsimd", "WD", lambda e, d=dst, s=src: e.dma_start(out=d[:], in_=s[:]))
        for cn, cv in CONSTS:
            S.step("vector", "VE", lambda e, t=consts[cn], v=cv: e.memset(t[:], v))
        ld_const = S.c["LD"]
        ve_const = S.c["VE"]

        stA = ExitStack()
        eva = stA.enter_context
        wq = eva(nc.sbuf_tensor("wq", [128, FC * DM], BF16))
        wk = eva(nc.sbuf_tensor("wk", [128, FC * DM], BF16))
        wv = eva(nc.sbuf_tensor("wv", [128, FC * DM], BF16))
        wo = eva(nc.sbuf_tensor("wo", [128, FC * DM], BF16))
        xw1 = [eva(nc.sbuf_tensor(f"xw1_{i}", [128, DM], F32)) for i in range(2)]
        xw2 = [eva(nc.sbuf_tensor(f"xw2_{i}", [128, DM], F32)) for i in range(2)]
        xc = [eva(nc.sbuf_tensor(f"xc_{i}", [128, DM], F32)) for i in range(2)]
        sqs = eva(nc.sbuf_tensor("sqs", [128, DM], BF16))
        xln1 = eva(nc.sbuf_tensor("xln1", [128, DM], F32))
        xln2 = eva(nc.sbuf_tensor("xln2", [128, DM], F32))
        xlnf = eva(nc.sbuf_tensor("xlnf", [128, FC * W], BF16))
        qfm = eva(nc.sbuf_tensor("qfm", [128, FC * CT], BF16))
        kfm = eva(nc.sbuf_tensor("kfm", [128, FC * W], BF16))
        vt1 = eva(nc.sbuf_tensor("vt1", [128, DM], BF16))
        vt2 = eva(nc.sbuf_tensor("vt2", [128, DM], BF16))
        exps = eva(nc.sbuf_tensor("exps", [128, W], BF16))
        pmsk = eva(nc.sbuf_tensor("pmsk", [128, W], BF16))
        pnA = eva(nc.sbuf_tensor("pnA", [128, CT], BF16))
        pnB = eva(nc.sbuf_tensor("pnB", [128, 64], BF16))
        ptA = eva(nc.sbuf_tensor("ptA", [128, CT], BF16))
        ptB = eva(nc.sbuf_tensor("ptB", [128, CT], BF16))
        attnf = eva(nc.sbuf_tensor("attnf", [128, FC * CT], BF16))
        x2sb = [eva(nc.sbuf_tensor(f"x2sb_{i}", [128, DM], F32)) for i in range(2)]
        stat = eva(nc.sbuf_tensor("statA", [128, 16], F32))
        psQV = eva(nc.psum_tensor("psQV", [128, 1024], F32))
        psK = eva(nc.psum_tensor("psK", [128, 1536], F32))
        psPT = eva(nc.psum_tensor("psPT", [128, 1024], BF16))
        psAT = eva(nc.psum_tensor("psAT", [128, 1024], F32))

        for dst, src in [(wq, D["qw"]), (wk, D["kw"]), (wv, D["vw"]), (wo, D["ow"])]:
            S.dma("gpsimd", "WD", lambda e, d=dst, s=src: e.dma_start(
                out=d[:], in_=s.rearrange("p c o -> p (c o)")))
        wd_attnw = S.c["WD"]

        def load_chunk(c):
            b = c % 2
            S.dma("sync", "LD", lambda e, c=c, b=b: e.dma_start(
                out=xw1[b][:], in_=D["xw"][c, 0:128, :]))
            S.dma("sync", "LD", lambda e, c=c, b=b: e.dma_start(
                out=xw2[b][0:64, :], in_=D["xw"][c, 128:192, :]))
            S.dma("sync", "LD", lambda e, c=c, b=b: e.dma_start(
                out=xc[b][:], in_=D["xw"][c, 32:160, :]))
            return S.c["LD"]

        ld_after = {0: load_chunk(0)}
        ve_chunk_done = {}
        S.wait("tensor", "WD", wd_attnw)
        S.wait("tensor", "LD", ld_const)
        S.wait("tensor", "VE", ve_const)

        for c in range(NCH):
            b = c % 2
            if c + 1 < NCH:
                if c >= 1:
                    S.wait("sync", "VE", ve_chunk_done[c - 1])
                ld_after[c + 1] = load_chunk(c + 1)

            S.wait("vector", "LD", ld_after[c])
            S.wait("scalar", "LD", ld_after[c])
            S.wait("vector", "WD", wd_attnw)
            S.wait("scalar", "VE", ve_chunk_done[c - 1] if c >= 1 else ve_const)

            ln_done = []
            for ti, (xwt, xlnt, kcolT, rows) in enumerate(
                    [(xw1[b], xln1, kvmT_a, 128), (xw2[b], xln2, kvmT_b, 64)]):
                r = slice(0, rows)
                S.step("vector", "VE", lambda e, x=xwt, r=r: e.reduce_sum(
                    stat[r, 0:1], x[r, :], axis=AX.X))
                s_sq = S.step_acc("scalar", "SE", lambda e, x=xwt, r=r: e.activation(
                    sqs[r, :], x[r, :], AF.Square, accum_out=stat[r, 1:2]))
                S.wait("vector", "SE", s_sq)
                S.stepb("vector", "VE", lambda e, r=r: e.tensor_scalar(
                    stat[r, 2:3], stat[r, 0:1], consts["i1024"][r, :], None, ALU.mult))
                S.step("vector", "VE", lambda e, r=r: e.tensor_scalar(
                    stat[r, 3:4], stat[r, 1:2], consts["i1024"][r, :], None, ALU.mult))
                S.stepb("vector", "VE", lambda e, r=r: e.tensor_tensor(
                    stat[r, 4:5], stat[r, 2:3], stat[r, 2:3], ALU.mult))
                v_var = S.stepb("vector", "VE", lambda e, r=r: e.tensor_tensor(
                    stat[r, 3:4], stat[r, 3:4], stat[r, 4:5], ALU.subtract))
                S.wait("scalar", "VE", v_var)
                s_sr = S.step("scalar", "SE", lambda e, r=r: e.activation(
                    stat[r, 0:1], stat[r, 3:4], AF.Sqrt, bias=consts["eps"][r, :]))
                S.wait("vector", "SE", s_sr)
                S.step_acc("vector", "VE", lambda e, r=r: e.reciprocal(stat[r, 8:9], stat[r, 0:1]))
                S.step("vector", "VE", lambda e, r=r, k=kcolT, c=c: e.tensor_tensor(
                    stat[r, 9:10], stat[r, 8:9], k[r, c:c + 1], ALU.mult))
                vdone = S.stepb("vector", "VE", lambda e, x=xwt, o=xlnt, r=r: e.tensor_scalar(
                    o[r, :], x[r, :], stat[r, 2:3], stat[r, 9:10], ALU.subtract, ALU.mult))
                ln_done.append(vdone)

            if debug and c == DBG_CHUNK:
                S.wait("sync", "VE", ln_done[-1])
                S.dma("sync", "ST", lambda e: e.dma_start(out=dbg_xln[:, 0:DM], in_=xln1[:]))
                S.dma("sync", "ST", lambda e: e.dma_start(out=dbg_xln[:, DM:2*DM], in_=xln2[:]))
                S.dma("sync", "ST", lambda e, b=b: e.dma_start(out=dbg_xln[:, 2*DM:3*DM], in_=xw2[b][:]))
                S.dma("sync", "ST", lambda e: e.dma_start(out=dbg_stat[:], in_=stat[:]))
                S.dma("sync", "ST", lambda e: e.dma_start(
                    out=dbg_xln[0:128, 3*DM-NCH:3*DM], in_=kvmT_a[:]))
                S.wait("vector", "ST", S.c["ST"])
            S.wait("tensor", "VE", ln_done[-1])
            ev_fm = []
            for fc_ in range(FC):
                po = (fc_ % 2) * 512
                if fc_ >= 2:
                    S.wait("tensor", "VE", ev_fm[fc_ - 2])
                S.on("tensor", lambda e, f=fc_, po=po: e.transpose(
                    psQV[:, po:po + 128], xln1[:, f * 128:(f + 1) * 128], identf[:]))
                t2 = S.step("tensor", "TE", lambda e, f=fc_, po=po: e.transpose(
                    psQV[:, po + 128:po + 192], xln2[0:64, f * 128:(f + 1) * 128],
                    identf[0:64, 0:64]))
                S.wait("vector", "TE", t2)
                ev_fm.append(S.step("vector", "VE", lambda e, f=fc_, po=po: e.tensor_copy(
                    xlnf[:, f * W:(f + 1) * W], psQV[:, po:po + 192])))

            ones_q = kvmrow[0:1, c * W + 32:c * W + 160]
            kv_row = kvmrow[0:1, c * W:(c + 1) * W]

            S.wait("tensor", "VE", ev_fm[-1])
            te_q = None
            for oc in range(FC):
                for kc in range(FC):
                    S.on("tensor", lambda e, oc=oc, kc=kc: e.matmul(
                        psQV[:, oc * 128:(oc + 1) * 128],
                        wq[:, kc * DM + oc * 128:kc * DM + (oc + 1) * 128],
                        xlnf[:, kc * W + 32:kc * W + 160],
                        start=(kc == 0), stop=False))
                te_q = S.step("tensor", "TE", lambda e, oc=oc, oq=ones_q: e.matmul(
                    psQV[:, oc * 128:(oc + 1) * 128], qb[0:1, oc * 128:(oc + 1) * 128],
                    oq, start=False, stop=True))
            S.wait("vector", "TE", te_q)
            ve_q = S.step("vector", "VE", lambda e: e.tensor_scalar(
                qfm[:], psQV[:], consts["inv8"][:], None, ALU.mult))

            ve_k = None
            for half in range(2):
                if half == 1:
                    S.wait("tensor", "VE", ve_k)
                te_k = None
                for o4 in range(4):
                    oc = half * 4 + o4
                    for kc in range(FC):
                        S.on("tensor", lambda e, oc=oc, kc=kc, o4=o4: e.matmul(
                            psK[:, o4 * 192:(o4 + 1) * 192],
                            wk[:, kc * DM + oc * 128:kc * DM + (oc + 1) * 128],
                            xlnf[:, kc * W:(kc + 1) * W],
                            start=(kc == 0), stop=False))
                    te_k = S.step("tensor", "TE", lambda e, oc=oc, kr=kv_row: e.matmul(
                        psK[:, (oc % 4) * 192:(oc % 4 + 1) * 192],
                        kb[0:1, oc * 128:(oc + 1) * 128],
                        kr, start=False, stop=True))
                S.wait("vector", "TE", te_k)
                ve_k = S.step("vector", "VE", lambda e, half=half: e.tensor_copy(
                    kfm[:, half * 4 * W:(half + 1) * 4 * W], psK[:, 0:4 * W]))

            S.wait("tensor", "VE", ve_q)
            ve_v = None
            for mt, (rows, colo) in enumerate([(128, 0), (64, 128)]):
                if mt == 1:
                    S.wait("tensor", "VE", ve_v)
                te_v = None
                for n2 in range(2):
                    for kc in range(FC):
                        S.on("tensor", lambda e, kc=kc, n2=n2, rows=rows, colo=colo: e.matmul(
                            psQV[0:rows, n2 * 512:(n2 + 1) * 512],
                            xlnf[:, kc * W + colo:kc * W + colo + rows],
                            wv[:, kc * DM + n2 * 512:kc * DM + (n2 + 1) * 512],
                            start=(kc == 0), stop=False))
                    te_v = S.step("tensor", "TE", lambda e, n2=n2, rows=rows, colo=colo,
                                  kr=kvmrow, c=c: e.matmul(
                        psQV[0:rows, n2 * 512:(n2 + 1) * 512],
                        kr[0:1, c * W + colo:c * W + colo + rows],
                        vb[0:1, n2 * 512:(n2 + 1) * 512], start=False, stop=True))
                S.wait("vector", "TE", te_v)
                vt = vt1 if mt == 0 else vt2
                ve_v = S.step("vector", "VE", lambda e, vt=vt, rows=rows: e.tensor_copy(
                    vt[0:rows, :], psQV[0:rows, :]))

            S.wait("tensor", "VE", ve_v)
            se_exp = {}
            ve_max = {}
            ve_ttr = {}
            ve_pt = {}
            te_avh = {}
            for h in range(NH):
                sco = 512 * (h % 2)
                fc_ = h // 2
                pb = 64 * (h % 2)
                nmc = 6 + (h % 2)
                if h >= 2:
                    S.wait("tensor", "SE", se_exp[h - 2])
                    S.wait("tensor", "VE", ve_max[h - 2])
                sc = S.step("tensor", "TE", lambda e, fc_=fc_, pb=pb, sco=sco: e.matmul(
                    psK[:, sco:sco + 192],
                    qfm[pb:pb + 64, fc_ * 128:(fc_ + 1) * 128],
                    kfm[pb:pb + 64, fc_ * W:(fc_ + 1) * W],
                    start=True, stop=True))
                S.wait("vector", "TE", sc)
                if h >= 2:
                    S.wait("vector", "SE", se_exp[h - 2])
                ve_max[h] = S.step("vector", "VE", lambda e, sco=sco, nmc=nmc: e.reduce_max(
                    stat[:, nmc:nmc + 1], psK[:, sco:sco + 192], axis=AX.X, negate=True))
                S.wait("scalar", "VE", ve_max[h])
                if h >= 1:
                    S.wait("scalar", "VE", ve_ttr[h - 1])
                se_exp[h] = S.step("scalar", "SE", lambda e, sco=sco, nmc=nmc: e.activation(
                    exps[:], psK[:, sco:sco + 192], AF.Exp, bias=stat[:, nmc:nmc + 1]))
                S.wait("vector", "SE", se_exp[h])
                ve_ttr[h] = S.step_acc("vector", "VE", lambda e: e.scalar_tensor_tensor(
                    pmsk[:], exps[:], consts["one"][:], bm[:],
                    ALU.mult, ALU.mult, accum_out=stat[:, 0:1]))
                S.step_acc("vector", "VE", lambda e: e.reciprocal(stat[:, 8:9], stat[:, 0:1]))
                if h >= 1:
                    S.wait("vector", "TE", te_avh[h - 1])
                S.step("vector", "VE", lambda e: e.tensor_scalar(
                    pnA[:], pmsk[:, 0:128], stat[:, 8:9], None, ALU.mult))
                ve_pn = S.step("vector", "VE", lambda e: e.tensor_scalar(
                    pnB[:], pmsk[:, 128:192], stat[:, 8:9], None, ALU.mult))
                if debug and c == DBG_CHUNK:
                    S.wait("sync", "VE", ve_pn)
                    S.dma("sync", "ST", lambda e, h=h: e.dma_start(
                        out=dbg_pn[:, h * CT:h * CT + 128], in_=pnA[:]))
                S.wait("tensor", "VE", ve_pn)
                if h >= 1:
                    S.wait("tensor", "VE", ve_pt[h - 1])
                S.on("tensor", lambda e: e.transpose(psPT[:, 0:128], pnA[:], identb[:]))
                tB = S.step("tensor", "TE", lambda e: e.transpose(
                    psPT[0:64, 128:256], pnB[:], identb[:]))
                S.wait("vector", "TE", tB)
                S.step("vector", "VE", lambda e: e.tensor_copy(ptA[:], psPT[:, 0:128]))
                ve_pt[h] = S.step("vector", "VE", lambda e: e.tensor_copy(
                    ptB[0:64, :], psPT[0:64, 128:256]))
                S.wait("tensor", "VE", ve_pt[h])
                tp = (0, pb)
                S.on("tensor", lambda e, h=h, fc_=fc_, pb=pb, tp=tp: e.matmul(
                    psAT[pb:pb + 64, fc_ * 128:(fc_ + 1) * 128],
                    vt1[:, h * 64:(h + 1) * 64], ptA[:],
                    start=True, stop=False, tile_position=tp))
                te_avh[h] = S.step("tensor", "TE", lambda e, h=h, fc_=fc_, pb=pb, tp=tp: e.matmul(
                    psAT[pb:pb + 64, fc_ * 128:(fc_ + 1) * 128],
                    vt2[0:64, h * 64:(h + 1) * 64], ptB[0:64, :],
                    start=False, stop=True, tile_position=tp))

            S.wait("vector", "TE", te_avh[NH - 1])
            ve_at = S.step("vector", "VE", lambda e: e.tensor_copy(attnf[:], psAT[:]))
            S.wait("tensor", "VE", ve_at)
            te_o = None
            for n2 in range(2):
                for kc in range(FC):
                    S.on("tensor", lambda e, kc=kc, n2=n2: e.matmul(
                        psAT[:, n2 * 512:(n2 + 1) * 512],
                        attnf[:, kc * 128:(kc + 1) * 128],
                        wo[:, kc * DM + n2 * 512:kc * DM + (n2 + 1) * 512],
                        start=(kc == 0), stop=False))
                te_o = S.step("tensor", "TE", lambda e, n2=n2, oq=ones_q: e.matmul(
                    psAT[:, n2 * 512:(n2 + 1) * 512], oq,
                    ob[0:1, n2 * 512:(n2 + 1) * 512], start=False, stop=True))
            S.wait("vector", "TE", te_o)
            ve_x2 = S.step("vector", "VE", lambda e, b=b: e.tensor_tensor(
                x2sb[b][:], psAT[:], xc[b][:], ALU.add))
            ve_chunk_done[c] = ve_x2
            if debug and c == DBG_CHUNK:
                S.wait("sync", "VE", ve_x2)
                S.dma("sync", "ST", lambda e: e.dma_start(out=dbg_xlnf[:], in_=xlnf[:]))
                S.dma("sync", "ST", lambda e: e.dma_start(out=dbg_qfm[:], in_=qfm[:]))
                S.dma("sync", "ST", lambda e: e.dma_start(out=dbg_kfm[:], in_=kfm[:]))
                S.dma("sync", "ST", lambda e: e.dma_start(out=dbg_vt[:, 0:DM], in_=vt1[:]))
                S.dma("sync", "ST", lambda e: e.dma_start(out=dbg_vt[:, DM:2*DM], in_=vt2[:]))
                S.dma("sync", "ST", lambda e: e.dma_start(out=dbg_at[:], in_=attnf[:]))
            S.wait("sync", "VE", ve_x2)
            S.dma("sync", "ST", lambda e, c=c, b=b: e.dma_start(
                out=x2_d[c, :, :], in_=x2sb[b][:]))

        S.flush(blk)
        st_x2 = S.c["ST"]
        stA.close()

        if PHASES < 1:
            return nc
        acc = ev(nc.sbuf_tensor("acc", [128, NCH * DM], F32))
        stFM = ExitStack()
        xfnf = stFM.enter_context(nc.sbuf_tensor("xfnf", [128, FC * TOK], BF16))
        stB = ExitStack()
        evb = stB.enter_context
        xfnt = evb(nc.sbuf_tensor("xfnt", [128, NCH * DM], BF16))
        x2r = [evb(nc.sbuf_tensor(f"x2r_{i}", [128, DM], F32)) for i in range(2)]
        xf32 = evb(nc.sbuf_tensor("xf32", [128, DM], F32))
        xtr32 = evb(nc.sbuf_tensor("xtr32", [128, 128], F32))
        sqs2 = evb(nc.sbuf_tensor("sqs2", [128, DM], BF16))
        statB = evb(nc.sbuf_tensor("statB", [128, 16], F32))
        lgt = evb(nc.sbuf_tensor("lgt", [128, E], F32))
        top8 = evb(nc.sbuf_tensor("top8", [128, 8], F32))
        junk8 = evb(nc.sbuf_tensor("junk8", [128, E], BF16))
        tmp8a = evb(nc.sbuf_tensor("tmp8a", [128, E], F32))
        tmp8b = evb(nc.sbuf_tensor("tmp8b", [128, E], BF16))
        ktile = evb(nc.sbuf_tensor("ktile", [128, E], BF16))
        slotf = evb(nc.sbuf_tensor("slotf", [128, E], F32))
        rowt = evb(nc.sbuf_tensor("rowt", [1, E], F32))
        bb2 = evb(nc.sbuf_tensor("bb2", [128, E], F32))
        colt = evb(nc.sbuf_tensor("colt", [128, 12], F32))
        zt = evb(nc.sbuf_tensor("zt", [128, FC * DM], BF16))
        psB = evb(nc.psum_tensor("psB", [128, 1024], F32))
        psLG = evb(nc.psum_tensor("psLG", [128, 512], F32))
        psRK = evb(nc.psum_tensor("psRK", [128, 512], F32))
        psB1 = evb(nc.psum_tensor("psB1", [128, 512], F32))
        psB2 = evb(nc.psum_tensor("psB2", [128, 512], F32))
        psSL = evb(nc.psum_tensor("psSL", [128, 512], F32))

        def load_x2(c):
            S.dma("sync", "LD", lambda e, c=c: e.dma_start(
                out=x2r[c % 2][:], in_=x2_d[c, :, :]))
            return S.c["LD"]
        S.wait("sync", "ST", st_x2)
        ld_x2 = {0: load_x2(0)}
        ve_b_done = {}
        v_lgt_prev = 0
        for c in range(NCH):
            b = c % 2
            if c + 1 < NCH:
                if c >= 1:
                    S.wait("sync", "VE", ve_b_done[c - 1])
                ld_x2[c + 1] = load_x2(c + 1)
            S.wait("vector", "LD", ld_x2[c])
            S.wait("scalar", "LD", ld_x2[c])
            if c >= 1:
                S.wait("scalar", "VE", ve_b_done[c - 1])
            S.step("vector", "VE", lambda e, b=b: e.reduce_sum(
                statB[:, 0:1], x2r[b][:], axis=AX.X))
            s_sq = S.step_acc("scalar", "SE", lambda e, b=b: e.activation(
                sqs2[:], x2r[b][:], AF.Square, accum_out=statB[:, 1:2]))
            S.wait("vector", "SE", s_sq)
            S.stepb("vector", "VE", lambda e: e.tensor_scalar(
                statB[:, 2:3], statB[:, 0:1], consts["i1024"][:], None, ALU.mult))
            S.step("vector", "VE", lambda e: e.tensor_scalar(
                statB[:, 3:4], statB[:, 1:2], consts["i1024"][:], None, ALU.mult))
            S.stepb("vector", "VE", lambda e: e.tensor_tensor(
                statB[:, 4:5], statB[:, 2:3], statB[:, 2:3], ALU.mult))
            v_var = S.stepb("vector", "VE", lambda e: e.tensor_tensor(
                statB[:, 3:4], statB[:, 3:4], statB[:, 4:5], ALU.subtract))
            S.wait("scalar", "VE", v_var)
            s_sr = S.step("scalar", "SE", lambda e: e.activation(
                statB[:, 0:1], statB[:, 3:4], AF.Sqrt, bias=consts["eps"][:]))
            S.wait("vector", "SE", s_sr)
            S.step_acc("vector", "VE", lambda e: e.reciprocal(statB[:, 8:9], statB[:, 0:1]))
            v_f32 = S.stepb("vector", "VE", lambda e, b=b: e.tensor_scalar(
                xf32[:], x2r[b][:], statB[:, 2:3], statB[:, 8:9], ALU.subtract, ALU.mult))
            v_bf = S.stepb("vector", "VE", lambda e, c=c: e.tensor_copy(
                xfnt[:, c * DM:(c + 1) * DM], xf32[:]))
            S.wait("scalar", "VE", v_f32)
            s_ss = S.step_acc("scalar", "SE", lambda e: e.activation(
                sqs2[:], xf32[:], AF.Square, accum_out=statB[:, 4:5]))
            S.wait("vector", "SE", s_ss)
            S.step("vector", "VE", lambda e: e.tensor_scalar(
                statB[:, 5:6], statB[:, 4:5], consts["i1024"][:], None, ALU.mult))
            v_rr = S.stepb("vector", "VE", lambda e: e.tensor_tensor(
                statB[:, 5:6], statB[:, 5:6], consts["epsr"][:], ALU.add))
            S.wait("scalar", "VE", v_rr)
            s_q = S.step("scalar", "SE", lambda e: e.activation(
                statB[:, 0:1], statB[:, 5:6], AF.Sqrt, bias=consts["zero"][:]))
            S.wait("vector", "SE", s_q)
            S.step_acc("vector", "VE", lambda e: e.reciprocal(statB[:, 8:9], statB[:, 0:1]))

            S.wait("tensor", "VE", v_f32)
            v_32 = {}
            te_lg = v_lgt_first = None
            for fc_ in range(FC):
                po = (fc_ % 2) * 512
                if fc_ >= 2:
                    S.wait("tensor", "VE", v_32[fc_ - 2])
                if fc_ == 0:
                    S.wait("tensor", "VE", v_lgt_prev)
                tt_ = S.step("tensor", "TE", lambda e, f=fc_, po=po: e.transpose(
                    psB[:, po:po + 128], xf32[:, f * 128:(f + 1) * 128], identf[:]))
                S.wait("vector", "TE", tt_)
                S.step("vector", "VE", lambda e, f=fc_, po=po, c=c: e.tensor_copy(
                    xfnf[:, f * TOK + c * 128:f * TOK + (c + 1) * 128],
                    psB[:, po:po + 128]))
                if fc_ >= 1:
                    S.wait("vector", "TE", te_lg)
                v_32[fc_] = S.step("vector", "VE", lambda e, po=po: e.tensor_copy(
                    xtr32[:], psB[:, po:po + 128]))
                S.wait("tensor", "VE", v_32[fc_])
                te_lg = S.step("tensor", "TE", lambda e, f=fc_: e.matmul(
                    psLG[:, 0:E], xtr32[:], rwT[:, f * E:(f + 1) * E],
                    start=(f == 0), stop=(f == FC - 1), skip_group_check=True))
            S.wait("vector", "TE", te_lg)
            S.step("vector", "VE", lambda e: e.tensor_scalar(
                lgt[:], psLG[:, 0:E], statB[:, 8:9], None, ALU.mult))
            S.bar("vector")
            S.step_acc("vector", "VE", lambda e: e.max(top8[:], lgt[:]))
            S.step("vector", "VE", lambda e, c=c: e.tensor_scalar(
                m1_all[:, c * E:(c + 1) * E], lgt[:], top8[:, 0:1], None, ALU.is_equal))
            S.step("vector", "VE", lambda e, c=c: e.tensor_scalar(
                m2_all[:, c * E:(c + 1) * E], lgt[:], top8[:, 1:2], None, ALU.is_equal))
            v_sel = S.stepb("vector", "VE", lambda e, c=c: e.tensor_tensor(
                sel_all[:, c * E:(c + 1) * E], m1_all[:, c * E:(c + 1) * E],
                m2_all[:, c * E:(c + 1) * E], ALU.max))
            v_lgt_prev = v_sel
            S.step("vector", "VE", lambda e, c=c: e.tensor_tensor(
                dall[:, c:c + 1], top8[:, 1:2], top8[:, 0:1], ALU.subtract))
            S.wait("tensor", "VE", v_sel)
            te_rk = S.step("tensor", "TE", lambda e, c=c: e.matmul(
                psRK[:, 0:E], tri[:], sel_all[:, c * E:(c + 1) * E], start=True, stop=True))
            S.wait("vector", "TE", te_rk)
            S.step("vector", "VE", lambda e, c=c: e.tensor_copy(
                rank_all[:, c * E:(c + 1) * E], psRK[:, 0:E]))
            v_rkcp = S.c["VE"]
            ve_b_done[c] = v_rkcp

        S.wait("tensor", "VE", S.c["VE"])
        te_cnt = S.step("tensor", "TE", lambda e: e.matmul(
            psRK[0:1, 0:128], onesc[:], sel_all[:], start=True, stop=True))
        S.wait("vector", "TE", te_cnt)
        v_cnt_all = S.step("vector", "VE", lambda e: e.tensor_copy(
            cntrow[:], psRK[0:1, 0:128]))
        S.wait("gpsimd", "VE", v_cnt_all)
        S.dma("gpsimd", "WD", lambda e: e.dma_start(
            out=cc_i.rearrange("c e -> (c e)")[None, :], in_=cntrow[:]))
        S.wait("gpsimd", "WD", S.c["WD"])
        S.step("gpsimd", "CCS", lambda e: e.collective_compute(
            "AllGather", ALU.bypass, replica_groups=[list(range(NCORE))],
            ins=[cc_i[:]], outs=[cc_o[:]]))
        S.wait("gpsimd", "CCS", S.c["CCS"])
        ag_ld = S.dma("gpsimd", "WD", lambda e: e.dma_start(out=counts_ag[:], in_=cc_o[:]))
        S.wait("tensor", "WD", ag_ld)
        v_run = S.step("vector", "VE", lambda e: e.memset(runoff[:], 0.0))
        ve_route_done = v_run

        for c in range(NCH):
            ce = slice(c * E, (c + 1) * E)
            S.wait("tensor", "VE", v_run)
            S.on("tensor", lambda e, c=c: e.matmul(
                psB1[:, 0:E], pickb[:, c * 128:(c + 1) * 128], counts_ag[:],
                start=True, stop=True))
            te_b2 = S.step("tensor", "TE", lambda e: e.matmul(
                psB2[:, 0:E], onesf[:], runoff[:], start=True, stop=True))
            S.wait("vector", "TE", te_b2)
            S.step("vector", "VE", lambda e, ce=ce: e.tensor_tensor(
                tmp8a[:], rank_all[:, ce], psB1[:, 0:E], ALU.add))
            S.stepb("vector", "VE", lambda e: e.tensor_scalar(
                tmp8b[:], tmp8a[:], consts["cap"][:], None, ALU.is_lt))
            v_kept = S.stepb("vector", "VE", lambda e, ce=ce: e.tensor_tensor(
                ktile[:], tmp8b[:], sel_all[:, ce], ALU.mult))
            S.wait("tensor", "VE", v_kept)
            te_sl = S.step("tensor", "TE", lambda e: e.matmul(
                psSL[:, 0:E], tri[:], ktile[:], start=True, stop=True))
            te_rt = S.step("tensor", "TE", lambda e: e.matmul(
                psRK[0:1, 0:E], onesc[:], ktile[:], start=True, stop=True))
            S.wait("vector", "TE", te_rt)
            S.step("vector", "VE", lambda e: e.tensor_copy(bb2[:], psB2[:, 0:E]))
            S.stepb("vector", "VE", lambda e: e.tensor_tensor(
                slotf[:], psSL[:, 0:E], bb2[:], ALU.add))
            S.step("vector", "VE", lambda e: e.tensor_copy(rowt[:], psRK[0:1, 0:E]))
            v_run = S.stepb("vector", "VE", lambda e: e.tensor_tensor(
                runoff[:], runoff[:], rowt[:], ALU.add))
            S.wait("scalar", "VE", ve_route_done)
            s_ed = S.step("scalar", "SE", lambda e, c=c: e.activation(
                colt[:, 4:5], dall[:, c:c + 1], AF.Sigmoid, bias=consts["zero"][:]))
            S.wait("vector", "SE", s_ed)
            S.stepb("vector", "VE", lambda e: e.tensor_tensor(
                colt[:, 5:6], consts["one"][:], colt[:, 4:5], ALU.subtract))
            for k, mall in ((0, m1_all), (1, m2_all)):
                S.step_acc("vector", "VE", lambda e, ce=ce, m=mall: e.scalar_tensor_tensor(
                    junk8[:], m[:, ce], consts["one"][:], iota8[:],
                    ALU.mult, ALU.mult, accum_out=colt[:, 1:2]))
                S.step_acc("vector", "VE", lambda e, ce=ce, m=mall: e.scalar_tensor_tensor(
                    junk8[:], m[:, ce], consts["one"][:], slotf[:],
                    ALU.mult, ALU.mult, accum_out=colt[:, 2:3]))
                S.step_acc("vector", "VE", lambda e, ce=ce, m=mall: e.scalar_tensor_tensor(
                    junk8[:], m[:, ce], consts["one"][:], ktile[:],
                    ALU.mult, ALU.mult, accum_out=colt[:, 3:4]))
                S.stepb("vector", "VE", lambda e: e.tensor_scalar(
                    colt[:, 6:7], colt[:, 2:3], consts["capP"][:], None, ALU.is_lt))
                S.stepb("vector", "VE", lambda e: e.tensor_tensor(
                    colt[:, 6:7], colt[:, 6:7], colt[:, 3:4], ALU.mult))
                S.stepb("vector", "VE", lambda e: e.tensor_scalar(
                    colt[:, 7:8], colt[:, 1:2], consts["capP"][:], colt[:, 2:3],
                    ALU.mult, ALU.add))
                S.stepb("vector", "VE", lambda e, c=c, k=k: e.tensor_tensor(
                    offs_f[:, c * 4 + 2 + k:c * 4 + 3 + k], colt[:, 7:8], colt[:, 6:7],
                    ALU.mult))
                S.stepb("vector", "VE", lambda e: e.tensor_scalar(
                    colt[:, 9:10], colt[:, 6:7], consts["negone"][:], consts["one"][:],
                    ALU.mult, ALU.add))
                S.stepb("vector", "VE", lambda e: e.tensor_tensor(
                    colt[:, 9:10], colt[:, 9:10], consts["huge"][:], ALU.mult))
                S.stepb("vector", "VE", lambda e, c=c, k=k: e.tensor_tensor(
                    offs_f[:, c * 4 + k:c * 4 + 1 + k],
                    offs_f[:, c * 4 + 2 + k:c * 4 + 3 + k], colt[:, 9:10], ALU.add))
                wcol = colt[:, 5:6] if k == 0 else colt[:, 4:5]
                ve_route_done = S.stepb("vector", "VE",
                    lambda e, c=c, k=k, wcol=wcol: e.tensor_tensor(
                    w01[:, c * 2 + k:c * 2 + k + 1], wcol, colt[:, 6:7], ALU.mult))
        S.bar("vector")
        v_offs = S.step_acc("vector", "VE", lambda e: e.tensor_copy(offs_u[:], offs_f[:]))
        v_zt = S.step("vector", "VE", lambda e: e.memset(zt[:], 0.0))
        S.wait("sync", "VE", v_zt)
        for a in range(4):
            S.dma("sync", "ST", lambda e, a=a: e.dma_start(
                out=xg_d[a * 1024:(a + 1) * 1024, :].rearrange("(p r) f -> p r f", p=128),
                in_=zt[:].rearrange("p (r f) -> p r f", f=DM)))
        st_zero = S.c["ST"]
        S.wait("gpsimd", "VE", v_offs)
        S.wait("gpsimd", "ST", st_zero)
        for c in range(NCH):
            for k in range(2):
                S.dma("gpsimd", "WD", lambda e, c=c, k=k: e.indirect_dma_start(
                    out=xg_d,
                    out_offset=IndirectOffsetOnAxis(ap=offs_u[:, c * 4 + k:c * 4 + k + 1], axis=0),
                    in_=xfnt[:, c * DM:(c + 1) * DM], in_offset=None,
                    bounds_check=NSLOT - 1, oob_is_err=False))
        wd_disp = S.c["WD"]

        if debug:
            S.wait("sync", "VE", S.c["VE"])
            for c in range(NCH):
                S.dma("sync", "ST", lambda e, c=c: e.dma_start(
                    out=dbg_xfn[c, :, :], in_=xfnt[:, c * DM:(c + 1) * DM]))
            S.dma("sync", "ST", lambda e: e.dma_start(out=dbg_off[:], in_=offs_f[:]))
            S.dma("sync", "ST", lambda e: e.dma_start(out=dbg_w[:], in_=w01[:]))
            S.dma("sync", "ST", lambda e: e.dma_start(out=dbg_dall[:], in_=dall[:]))
            S.dma("sync", "ST", lambda e: e.dma_start(out=dbg_cag[:], in_=counts_ag[:]))
            S.wait("sync", "ST", S.c["ST"])

        ve_end_B = S.c["VE"]
        S.flush(blk)
        stB.close()

        if PHASES < 2:
            stFM.close()
            return nc
        stC = ExitStack()
        evc = stC.enter_context
        sgb = evc(nc.sbuf_tensor("sgb", [128, FC * 1024], BF16))
        sub = evc(nc.sbuf_tensor("sub", [128, FC * 1024], BF16))
        sdb = evc(nc.sbuf_tensor("sdb", [128, FC * 1024], BF16))
        hblk = evc(nc.sbuf_tensor("hblk", [128, FC * TOK], BF16))
        gs_t = evc(nc.sbuf_tensor("gs_t", [128, TOK], BF16))
        psG = evc(nc.psum_tensor("psG", [128, TOK], F32))
        psU = evc(nc.psum_tensor("psU", [128, TOK], F32))

        S.wait("tensor", "VE", ve_end_B)
        S.wait("gpsimd", "WD", wd_disp)
        te_gu_done = 0
        te_sd_done = 0
        ve_hev = 0
        se_g = 0
        ve_acc = 0
        for bb in range(CBLOCKS):
            S.wait("gpsimd", "TE", te_gu_done)
            S.dma("gpsimd", "WD", lambda e, bb=bb: e.dma_start(
                out=sgb[:], in_=D["sg"][bb].rearrange("p c o -> p (c o)")))
            wd_su = S.dma("gpsimd", "WD", lambda e, bb=bb: e.dma_start(
                out=sub[:], in_=D["su"][bb].rearrange("p c o -> p (c o)")))
            S.wait("gpsimd", "TE", te_sd_done)
            wd_sd = S.dma("gpsimd", "WD", lambda e, bb=bb: e.dma_start(
                out=sdb[:], in_=D["sd"][bb].rearrange("p c o -> p (c o)")))
            S.wait("tensor", "WD", wd_su)
            for i in range(FC):
                S.wait("tensor", "SE", se_g)
                te_g = None
                for ts4 in range(4):
                    for kc in range(FC):
                        fn = lambda e, i=i, ts4=ts4, kc=kc: e.matmul(
                            psG[:, ts4 * 512:(ts4 + 1) * 512],
                            sgb[:, kc * 1024 + i * 128:kc * 1024 + (i + 1) * 128],
                            xfnf[:, kc * TOK + ts4 * 512:kc * TOK + (ts4 + 1) * 512],
                            start=(kc == 0), stop=(kc == FC - 1))
                        if ts4 == 3 and kc == FC - 1:
                            te_g = S.step("tensor", "TE", fn)
                        else:
                            S.on("tensor", fn)
                S.wait("tensor", "VE", ve_hev)
                te_u = None
                for ts4 in range(4):
                    for kc in range(FC):
                        fn = lambda e, i=i, ts4=ts4, kc=kc: e.matmul(
                            psU[:, ts4 * 512:(ts4 + 1) * 512],
                            sub[:, kc * 1024 + i * 128:kc * 1024 + (i + 1) * 128],
                            xfnf[:, kc * TOK + ts4 * 512:kc * TOK + (ts4 + 1) * 512],
                            start=(kc == 0), stop=(kc == FC - 1))
                        if ts4 == 3 and kc == FC - 1:
                            te_u = S.step("tensor", "TE", fn)
                        else:
                            S.on("tensor", fn)
                S.wait("scalar", "TE", te_g)
                S.wait("scalar", "VE", ve_hev)
                se_g = S.step("scalar", "SE", lambda e: e.activation(
                    gs_t[:], psG[:], AF.Silu))
                S.wait("vector", "TE", te_u)
                S.wait("vector", "SE", se_g)
                ve_hev = S.step("vector", "VE", lambda e, i=i: e.tensor_tensor(
                    hblk[:, i * TOK:(i + 1) * TOK], psU[:], gs_t[:], ALU.mult))
            te_gu_done = te_u
            S.wait("tensor", "WD", wd_sd)
            S.wait("tensor", "VE", ve_hev)
            S.wait("tensor", "SE", se_g)
            for rnd in range(4):
                if rnd >= 1:
                    S.wait("tensor", "VE", ve_acc)
                te_sd = None
                for tc in range(4):
                    tch = rnd * 4 + tc
                    pst = psG if tc < 2 else psU
                    po = (tc % 2) * 1024
                    for n2 in range(2):
                        for i in range(FC):
                            fn = lambda e, i=i, tch=tch, n2=n2, pst=pst, po=po: e.matmul(
                                pst[:, po + n2 * 512:po + (n2 + 1) * 512],
                                hblk[:, i * TOK + tch * 128:i * TOK + (tch + 1) * 128],
                                sdb[:, i * 1024 + n2 * 512:i * 1024 + (n2 + 1) * 512],
                                start=(i == 0), stop=(i == FC - 1))
                            if tc == 3 and n2 == 1 and i == FC - 1:
                                te_sd = S.step("tensor", "TE", fn)
                            else:
                                S.on("tensor", fn)
                S.wait("vector", "TE", te_sd)
                for tc in range(4):
                    tch = rnd * 4 + tc
                    pst = psG if tc < 2 else psU
                    po = (tc % 2) * 1024
                    if bb == 0:
                        ve_acc = S.step("vector", "VE",
                                        lambda e, tch=tch, pst=pst, po=po: e.tensor_copy(
                            acc[:, tch * DM:(tch + 1) * DM], pst[:, po:po + 1024]))
                    else:
                        ve_acc = S.step("vector", "VE",
                                        lambda e, tch=tch, pst=pst, po=po: e.tensor_tensor(
                            acc[:, tch * DM:(tch + 1) * DM], pst[:, po:po + 1024],
                            acc[:, tch * DM:(tch + 1) * DM], ALU.add))
            te_sd_done = S.c["TE"]
            S.wait("tensor", "VE", ve_acc)
        S.flush(blk)
        stC.close()
        stFM.close()

        if debug:
            S.wait("sync", "VE", S.c["VE"])
            for c in range(NCH):
                S.dma("sync", "ST", lambda e, c=c: e.dma_start(
                    out=dbg_acc[c, :, :], in_=acc[:, c * DM:(c + 1) * DM]))
            S.wait("sync", "ST", S.c["ST"])
            S.flush(blk)

        if PHASES < 3:
            return nc
        stD = ExitStack()
        evd = stD.enter_context
        rgb = evd(nc.sbuf_tensor("rgb", [128, FC * 1024], BF16))
        rub = evd(nc.sbuf_tensor("rub", [128, FC * 1024], BF16))
        rdb = evd(nc.sbuf_tensor("rdb", [128, FC * 1024], BF16))
        xge = [evd(nc.sbuf_tensor(f"xge_{i}", [128, FC * P], BF16)) for i in range(2)]
        he = evd(nc.sbuf_tensor("he", [128, 32 * P], BF16))
        gse = evd(nc.sbuf_tensor("gse", [128, P], BF16))
        eosb = evd(nc.sbuf_tensor("eosb", [128, DM], BF16))
        psD = evd(nc.psum_tensor("psD", [128, 4096], F32))

        te_C_done = te_sd_done
        ve_C_done = ve_acc
        se_C_done = se_g
        S.wait("sync", "WD", wd_disp)
        S.wait("sync", "TE", te_C_done)
        S.wait("sync", "VE", ve_C_done)
        S.wait("gpsimd", "TE", te_C_done)
        S.wait("gpsimd", "VE", ve_C_done)

        def load_xge(ee):
            for fc_ in range(FC):
                S.dma("sync", "LD", lambda e, ee=ee, fc_=fc_: e.dma_start(
                    out=xge[ee % 2][:, fc_ * P:(fc_ + 1) * P],
                    in_=xg_d[ee * P:(ee + 1) * P, fc_ * 128:(fc_ + 1) * 128],
                    transpose=True))
            return S.c["LD"]
        ld_xge = {0: load_xge(0)}
        te_gu_exp = {}
        te_last_eout = te_C_done
        ve_eo_ev = ve_C_done
        se_ge = se_C_done
        for ee in range(E):
            if ee + 1 < E:
                if ee >= 1:
                    S.wait("sync", "TE", te_gu_exp[ee - 1])
                ld_xge[ee + 1] = load_xge(ee + 1)
            te_gu_e = te_last_eout
            for rb in range(4):
                S.wait("gpsimd", "TE", te_gu_e)
                S.dma("gpsimd", "WD", lambda e, ee=ee, rb=rb: e.dma_start(
                    out=rgb[:], in_=D["rg"][ee, rb].rearrange("p c o -> p (c o)")))
                wd_ru = S.dma("gpsimd", "WD", lambda e, ee=ee, rb=rb: e.dma_start(
                    out=rub[:], in_=D["ru"][ee, rb].rearrange("p c o -> p (c o)")))
                S.wait("tensor", "WD", wd_ru)
                S.wait("tensor", "LD", ld_xge[ee])
                if rb == 0:
                    S.wait("tensor", "TE", te_last_eout)
                for i in range(FC):
                    irc = rb * FC + i
                    S.wait("tensor", "SE", se_ge)
                    te_g = None
                    for kc in range(FC):
                        fn = lambda e, i=i, kc=kc, ee=ee: e.matmul(
                            psD[:, 0:512],
                            rgb[:, kc * 1024 + i * 128:kc * 1024 + (i + 1) * 128],
                            xge[ee % 2][:, kc * P:(kc + 1) * P],
                            start=(kc == 0), stop=(kc == FC - 1))
                        if kc == FC - 1:
                            te_g = S.step("tensor", "TE", fn)
                        else:
                            S.on("tensor", fn)
                    S.wait("tensor", "VE", ve_eo_ev)
                    te_u = None
                    for kc in range(FC):
                        fn = lambda e, i=i, kc=kc, ee=ee: e.matmul(
                            psD[:, 512:1024],
                            rub[:, kc * 1024 + i * 128:kc * 1024 + (i + 1) * 128],
                            xge[ee % 2][:, kc * P:(kc + 1) * P],
                            start=(kc == 0), stop=(kc == FC - 1))
                        if kc == FC - 1:
                            te_u = S.step("tensor", "TE", fn)
                        else:
                            S.on("tensor", fn)
                    S.wait("scalar", "TE", te_g)
                    S.wait("scalar", "VE", ve_eo_ev)
                    se_ge = S.step("scalar", "SE", lambda e: e.activation(
                        gse[:], psD[:, 0:512], AF.Silu))
                    S.wait("vector", "TE", te_u)
                    S.wait("vector", "SE", se_ge)
                    ve_eo_ev = S.step("vector", "VE", lambda e, irc=irc: e.tensor_tensor(
                        he[:, irc * P:(irc + 1) * P], psD[:, 512:1024], gse[:], ALU.mult))
                te_gu_e = S.c["TE"]
            te_gu_exp[ee] = te_gu_e
            S.wait("tensor", "VE", ve_eo_ev)
            S.wait("tensor", "SE", se_ge)
            te_eo = te_gu_e
            for rb in range(4):
                S.wait("gpsimd", "TE", te_eo)
                wd_rd = S.dma("gpsimd", "WD", lambda e, ee=ee, rb=rb: e.dma_start(
                    out=rdb[:], in_=D["rd"][ee, rb].rearrange("p c o -> p (c o)")))
                S.wait("tensor", "WD", wd_rd)
                te_eo = None
                for st4 in range(4):
                    for n2 in range(2):
                        for i in range(FC):
                            irc = rb * FC + i
                            fn = lambda e, irc=irc, st4=st4, n2=n2, i=i: e.matmul(
                                psD[:, st4 * 1024 + n2 * 512:st4 * 1024 + (n2 + 1) * 512],
                                he[:, irc * P + st4 * 128:irc * P + (st4 + 1) * 128],
                                rdb[:, i * 1024 + n2 * 512:i * 1024 + (n2 + 1) * 512],
                                start=(irc == 0), stop=(irc == 31))
                            if st4 == 3 and n2 == 1 and i == FC - 1:
                                te_eo = S.step("tensor", "TE", fn)
                            else:
                                S.on("tensor", fn)
            te_last_eout = te_eo
            S.wait("vector", "TE", te_eo)
            for st4 in range(4):
                vv = S.step("vector", "VE", lambda e, st4=st4: e.tensor_copy(
                    eosb[:], psD[:, st4 * 1024:(st4 + 1) * 1024]))
                S.wait("sync", "VE", vv)
                st_now = S.dma("sync", "ST", lambda e, ee=ee, st4=st4: e.dma_start(
                    out=eo_d[ee * P + st4 * 128:ee * P + (st4 + 1) * 128, :], in_=eosb[:]))
                S.wait("vector", "ST", st_now)
        st_eo = S.c["ST"]
        S.flush(blk)
        stD.close()

        if PHASES < 4:
            return nc
        stE = ExitStack()
        eve = stE.enter_context
        ctile = [eve(nc.sbuf_tensor(f"ct_{i}", [128, DM], BF16)) for i in range(4)]
        x2f = [eve(nc.sbuf_tensor(f"x2f_{i}", [128, DM], F32)) for i in range(2)]
        rt = eve(nc.sbuf_tensor("rt", [128, DM], F32))
        rt2 = eve(nc.sbuf_tensor("rt2", [128, DM], F32))
        ot = [eve(nc.sbuf_tensor(f"ot_{i}", [128, DM], F32)) for i in range(2)]

        S.wait("gpsimd", "ST", st_eo)
        S.wait("sync", "ST", st_eo)
        S.wait("sync", "TE", te_last_eout)
        S.wait("sync", "VE", S.c["VE"])
        ld_fin = {}
        wd_fin = {}
        def load_fin(c):
            S.dma("sync", "LD", lambda e, c=c: e.dma_start(
                out=x2f[c % 2][:], in_=x2_d[c, :, :]))
            ld_fin[c] = S.c["LD"]
            for k in range(2):
                S.dma("gpsimd", "WD", lambda e, c=c, k=k: e.indirect_dma_start(
                    out=ctile[(c % 2) * 2 + k][:], out_offset=None,
                    in_=eo_d,
                    in_offset=IndirectOffsetOnAxis(
                        ap=offs_u[:, c * 4 + 2 + k:c * 4 + 3 + k], axis=0)))
            wd_fin[c] = S.c["WD"]
        load_fin(0)
        ve_fin = {}
        for c in range(NCH):
            b = c % 2
            if c + 1 < NCH:
                if c >= 1:
                    S.wait("sync", "VE", ve_fin[c - 1])
                    S.wait("gpsimd", "VE", ve_fin[c - 1])
                load_fin(c + 1)
            S.wait("vector", "LD", ld_fin[c])
            S.wait("vector", "WD", wd_fin[c])
            S.step("vector", "VE", lambda e, c=c, b=b: e.tensor_scalar(
                rt[:], ctile[b * 2][:], w01[:, c * 2:c * 2 + 1], None, ALU.mult))
            S.stepb("vector", "VE", lambda e, c=c, b=b: e.scalar_tensor_tensor(
                rt2[:], ctile[b * 2 + 1][:], w01[:, c * 2 + 1:c * 2 + 2], rt[:],
                ALU.mult, ALU.add))
            S.stepb("vector", "VE", lambda e, c=c: e.tensor_tensor(
                rt[:], rt2[:], acc[:, c * DM:(c + 1) * DM], ALU.add))
            v_o = S.stepb("vector", "VE", lambda e, c=c, b=b: e.tensor_tensor(
                ot[b][:], rt[:], x2f[b][:], ALU.add))
            ve_fin[c] = v_o
            S.wait("sync", "VE", v_o)
            S.dma("sync", "ST", lambda e, c=c, b=b: e.dma_start(
                out=out_d[c, :, :], in_=ot[b][:]))
        S.wait("sync", "ST", S.c["ST"])
        S.flush(blk)
        stE.close()

    return nc



def _img_w(wt):
    return np.ascontiguousarray(wt.T.reshape(FC, 128, -1).transpose(1, 0, 2))

def _img_sg(sg_):
    return np.ascontiguousarray(sg_.reshape(FC, 128, 8, 1024).transpose(2, 1, 0, 3))

def _img_sd(sd_):
    return np.ascontiguousarray(sd_.reshape(8, 8, 128, 1024).transpose(0, 2, 1, 3))

def _img_rg(rg_):
    return np.ascontiguousarray(rg_.reshape(E, FC, 128, 4, 1024).transpose(0, 3, 2, 1, 4))

def _img_rd(rd_):
    return np.ascontiguousarray(rd_.reshape(E, 4, 8, 128, 1024).transpose(0, 1, 3, 2, 4))


def stage_inputs(inp):
    x = np.asarray(inp["L0_attn_hidden_states"], dtype=np.float32).reshape(-1, DM)
    jj = np.arange(W)[None, :] - np.arange(128)[:, None]
    shared = {
        "qw": _img_w(np.asarray(inp["L0_attn_q_w"], np.float32)),
        "kw": _img_w(np.asarray(inp["L0_attn_k_w"], np.float32)),
        "vw": _img_w(np.asarray(inp["L0_attn_v_w"], np.float32)),
        "ow": _img_w(np.asarray(inp["L0_attn_o_w"], np.float32)),
        "qb": np.asarray(inp["L0_attn_q_b"], np.float32)[None, :],
        "kb": np.asarray(inp["L0_attn_k_b"], np.float32)[None, :],
        "vb": np.asarray(inp["L0_attn_v_b"], np.float32)[None, :],
        "ob": np.asarray(inp["L0_attn_o_b"], np.float32)[None, :],
        "rw": np.ascontiguousarray(np.asarray(inp["L0_ffn_router_w"], np.float32)
                                   .T.reshape(FC, 128, E).transpose(1, 0, 2)),
        "sg": _img_sg(np.asarray(inp["L0_ffn_sg"], np.float32)),
        "su": _img_sg(np.asarray(inp["L0_ffn_su"], np.float32)),
        "sd": _img_sd(np.asarray(inp["L0_ffn_sd"], np.float32)),
        "rg": _img_rg(np.asarray(inp["L0_ffn_rg"], np.float32)),
        "ru": _img_rg(np.asarray(inp["L0_ffn_ru"], np.float32)),
        "rd": _img_rd(np.asarray(inp["L0_ffn_rd"], np.float32)),
        "identf": np.eye(128, dtype=np.float32),
        "identb": np.eye(128, dtype=ml_dtypes.bfloat16),
        "tri": np.triu(np.ones((128, 128), np.float32), 1).astype(ml_dtypes.bfloat16),
        "onesf": np.ones((1, 128), np.float32),
        "onesc": np.ones((128, 1), ml_dtypes.bfloat16),
        "iota8": np.broadcast_to(np.arange(E, dtype=np.float32), (128, E)).copy(),
        "bm": ((jj >= 0) & (jj <= 64)).astype(ml_dtypes.bfloat16),
    }
    in_maps = []
    for core in range(NCORE):
        m = dict(shared)
        xw = np.zeros((NCH, W, DM), np.float32)
        kvm = np.zeros((NCH, W), np.float32)
        for k in range(NCH):
            g = 8 * k + core
            t0 = g * CT
            s = t0 // SEQ
            lo, hi = s * SEQ, (s + 1) * SEQ
            a, bnd = max(t0 - HALO, lo), min(t0 + CT + HALO, hi)
            xw[k, a - (t0 - HALO):bnd - (t0 - HALO), :] = x[a:bnd]
            kvm[k, a - (t0 - HALO):bnd - (t0 - HALO)] = 1.0
        m["xw"] = xw
        m["kvm"] = kvm
        m["kvmt"] = np.ascontiguousarray(kvm.T)
        rr = np.arange(128) // NCH
        cl_ = np.arange(128) % NCH
        gl = 8 * cl_ + rr
        pickb_ = np.zeros((NCH, 128, 128), np.float32)
        for cl in range(NCH):
            pickb_[cl, :, :] = (gl < 8 * cl + core).astype(np.float32)[:, None]
        m["pickb"] = pickb_
        in_maps.append(m)
    return in_maps


_nc_cache = {}

def kernel(**inputs):
    in_maps = stage_inputs(inputs)
    key = ("nc", DEBUG)
    if key not in _nc_cache:
        _nc_cache[key] = build_nc(debug=DEBUG)
    nc = _nc_cache[key]
    res = run_bass_kernel_spmd(nc, in_maps, core_ids=list(range(NCORE)))
    out = np.empty((128, CT, DM), np.float32)
    for core in range(NCORE):
        o = np.asarray(res.results[core]["out"])
        for k in range(NCH):
            out[8 * k + core] = o[k]
    if DEBUG:
        kernel._last = res
    return out.reshape(4, SEQ, DM)
